# version 12
# baseline (speedup 1.0000x reference)
import numpy as np
import ml_dtypes
import concourse.bass as bass
import concourse.mybir as mybir
from concourse.bass_utils import run_bass_kernel_spmd
from contextlib import ExitStack

F32 = mybir.dt.float32
F32R = mybir.dt.float32r
BF16 = mybir.dt.bfloat16
F16 = mybir.dt.float16
F8 = mybir.dt.float8e4
PM = mybir.MatmulPerfMode.DoubleRow
I32 = mybir.dt.int32
I16 = mybir.dt.int16
AF = mybir.ActivationFunctionType
ALU = mybir.AluOpType
AX = mybir.AxisListType

NCORE = 8
B = 16
TLOC = 256
D = 1024
M = 256
H = 4096
CH = 32
NCHUNK = TLOC // CH
TOKC = CH * B
NTILE = (TLOC * B) // 128
LAG = 8
RING = 9
TPC = NTILE // NCHUNK
DEC = 0.99
DEC32 = float(np.float32(DEC) ** np.float32(32))
MAGIC = 12582912.0
EXPMASK = -8388608
CLAMP_MIN = float(2.0 ** -10)


def quant_f32(v, axis, src32, dst16, habs16, view3, bc, view3_32, bc32, l1, am_bf, am32,
              acl, apw, asc, ain, asc16, extra_mul=None, extra_add=None,
              amax_src16=None, fuse=None):
    if amax_src16 is not None:
        v.tensor_reduce(am32.unsqueeze(2), view3(amax_src16), AX.X, ALU.max, apply_absolute_value=True)
    else:
        v.tensor_reduce(am32.unsqueeze(2), view3_32(src32), AX.X, ALU.max, apply_absolute_value=True)
    v.drain()
    v.tensor_single_scalar(acl, am32, CLAMP_MIN, ALU.max)
    v.drain()
    v.tensor_single_scalar(apw.bitcast(I32), acl.bitcast(I32), EXPMASK, ALU.bitwise_and)
    v.drain()
    v.tensor_single_scalar(asc.bitcast(I32), apw.bitcast(I32), 3 << 23, ALU.subtract)
    v.drain()
    v.tensor_scalar(ain.bitcast(I32), asc.bitcast(I32), -1, 254 << 23, ALU.mult, ALU.add)
    v.drain()
    v.tensor_copy(asc16, asc)
    v.drain()
    if fuse is not None:
        srstd_col, negmr_col, rinv_b, nminv_b = fuse
        v.tensor_scalar(rinv_b, ain, srstd_col, None, ALU.mult)
        v.tensor_scalar(nminv_b, ain, negmr_col, None, ALU.mult)
        v.drain()
        extra_mul, extra_add = rinv_b, nminv_b
    if extra_mul is not None:
        v.tensor_tensor(view3_32(src32), view3_32(src32), bc32(extra_mul), ALU.mult)
        v.drain()
        v.tensor_tensor(view3_32(src32), view3_32(src32), bc32(extra_add), ALU.add)
        v.drain()
    else:
        v.tensor_tensor(view3_32(src32), view3_32(src32), bc32(ain), ALU.mult)
        v.drain()
    v.tensor_scalar(habs16, src32, MAGIC, MAGIC, ALU.add, ALU.subtract)
    v.drain()
    v.tensor_tensor(view3(dst16), view3(habs16), bc(asc16), ALU.mult)
    v.drain()


def quant_chain(v, axis, src_bf, dst_bf, view3, bc, l1, am_bf, am32, acl, apw, asc, ain,
                asc16=None, ain16=None, habs=None, pre_round_wait=None):
    if habs is None:
        habs = dst_bf
    v.tensor_reduce(am32.unsqueeze(2), view3(src_bf), AX.X, ALU.max, apply_absolute_value=True)
    v.drain()
    v.tensor_single_scalar(acl, am32, CLAMP_MIN, ALU.max)
    v.drain()
    v.tensor_single_scalar(apw.bitcast(I32), acl.bitcast(I32), EXPMASK, ALU.bitwise_and)
    v.drain()
    v.tensor_single_scalar(asc.bitcast(I32), apw.bitcast(I32), 3 << 23, ALU.subtract)
    v.drain()
    v.tensor_scalar(ain.bitcast(I32), asc.bitcast(I32), -1, 254 << 23, ALU.mult, ALU.add)
    v.drain()
    v.tensor_copy(asc16, asc)
    v.tensor_copy(ain16, ain)
    v.drain()
    v.tensor_tensor(view3(habs), view3(src_bf), bc(ain16), ALU.mult)
    v.drain()
    if pre_round_wait is not None:
        pre_round_wait()
    v.tensor_scalar(src_bf, habs, MAGIC, MAGIC, ALU.add, ALU.subtract)
    v.drain()
    v.tensor_tensor(view3(dst_bf), view3(src_bf), bc(asc16), ALU.mult)
    v.drain()


def build_nc():
    nc = bass.Bass("TRN2", target_bir_lowering=False)

    x_h = nc.dram_tensor("x", [B, TLOC, D], F32, kind="ExternalInput")
    wqt_h = nc.dram_tensor("wqt", [D, M], F32, kind="ExternalInput")
    wkt_h = nc.dram_tensor("wkt", [D, M], F32, kind="ExternalInput")
    wvt_h = nc.dram_tensor("wvt", [D, M], F32, kind="ExternalInput")
    wot_h = nc.dram_tensor("wot", [M, D], F32, kind="ExternalInput")
    m0t_h = nc.dram_tensor("m0t", [M, M], F32, kind="ExternalInput")
    gate_h = nc.dram_tensor("gate", [1, 1], F32, kind="ExternalInput")
    wfc_h = nc.dram_tensor("wfc", [H, D], F32, kind="ExternalInput")
    bfc_h = nc.dram_tensor("bfc", [1, H], F32, kind="ExternalInput")
    wproj_h = nc.dram_tensor("wproj", [D, H], F32, kind="ExternalInput")
    bproj_h = nc.dram_tensor("bproj", [1, D], F32, kind="ExternalInput")
    dq_h = nc.dram_tensor("dq", [128, TOKC], F16, kind="ExternalInput")
    dk_h = nc.dram_tensor("dk", [128, TOKC], F16, kind="ExternalInput")
    mask_h = nc.dram_tensor("maskt", [128, TPC, TOKC], F16, kind="ExternalInput")
    gvt_h = nc.dram_tensor("gvt", [128, 9], F32, kind="ExternalInput")
    ones_h = nc.dram_tensor("onesr", [1, 128], F32, kind="ExternalInput")
    idn_h = nc.dram_tensor("idn", [128, 128], F16, kind="ExternalInput")
    out_h = nc.dram_tensor("out", [B, TLOC, D], F32, kind="ExternalOutput")

    zin_h = nc.dram_tensor("zin", [M, M], F16)
    zout_h = nc.dram_tensor("zout", [NCORE * M, M], F16, addr_space="Shared")
    ob_h = nc.dram_tensor("obst", [NTILE, 128, D], F16)
    qt_h = nc.dram_tensor("qtst", [NCHUNK, 128, 2, TOKC], F16)

    ctx = ExitStack()
    sb = lambda n, s, d: ctx.enter_context(nc.sbuf_tensor(n, s, d))
    sem = lambda n: ctx.enter_context(nc.semaphore(n))

    with ctx:
        rT = sb("rT", [128, 2, NTILE * 128], F16)
        U_all = sb("U_all", [128, 2, NCHUNK, M], F16)
        wot_bf = sb("wot_bf", [128, 2, D], F16)
        zbf = sb("zbf", [128, 2, M], F16)
        barena = sb("barena", [1, 5376], F16)
        sgate_col = sb("sgate_col", [128, 1], F32)
        idn_sb = sb("idn_sb", [128, 128], F16)
        eps_col = sb("eps_col", [128, 1], F32)
        bfc_r = barena[0:1, 0:H]
        bpr_r = barena[0:1, H:H + D]
        ones_r = barena[0:1, H + D:H + D + 128]
        sig_b = barena[0:1, H + D + 128:H + D + 129]

        swl = sem("swl"); sSIG = sem("sSIG"); sGMM = sem("sGMM"); sWCV = sem("sWCV")
        sWQD = sem("sWQD"); sWT = sem("sWT"); sW8 = sem("sW8"); sWP8 = sem("sWP8")
        sBNA = sem("sBNA"); sRSD = sem("sRSD"); sNM = sem("sNM"); sAPL = sem("sAPL")
        sXQ = sem("sXQ"); sXQT = sem("sXQT"); sXQ8 = sem("sXQ8")
        sXNM = sem("sXNM"); sXNE = sem("sXNE")
        sQMM = sem("sQMM"); sKMM = sem("sKMM"); sVMM = sem("sVMM"); sAMM = sem("sAMM")
        sRMM = sem("sRMM"); sUMM = sem("sUMM")
        sQE = sem("sQE"); sKE = sem("sKE"); sVE = sem("sVE"); sAE = sem("sAE")
        sRE = sem("sRE"); sUE = sem("sUE"); sKTC = sem("sKTC")
        sH1 = sem("sH1"); sGEL = sem("sGEL"); sHQ = sem("sHQ"); sHQT = sem("sHQT")
        sH8 = sem("sH8"); sPOM = sem("sPOM"); sPOE = sem("sPOE")
        sZC = sem("sZC"); sCCD = sem("sCCD"); sCC = sem("sCC")
        sSTI = sem("sSTI"); sSTB = sem("sSTB"); sRIM = sem("sRIM"); sRIE = sem("sRIE")
        sSMM = sem("sSMM"); sCMB = sem("sCMB"); sOUT = sem("sOUT")
        sTAN = sem("sTAN"); sSIL = sem("sSIL"); sTLD = sem("sTLD"); sOBL = sem("sOBL")
        sQTS = sem("sQTS"); sQTL = sem("sQTL")
        swlg = sem("swlg"); sOBS = sem("sOBS")
        swfl2 = [sem("swflA"), sem("swflB")]
        swt2 = [sem("swtA"), sem("swtB")]
        shqt2 = [sem("shqtA"), sem("shqtB")]
        sqts2 = [sem("sqtsA"), sem("sqtsB")]
        sqtl2 = [sem("sqtlA"), sem("sqtlB")]
        sobl2 = [sem("soblA"), sem("soblB")]
        sCCz = [sem("sCCzA"), sem("sCCzB")]
        sxlp = [sem("sxlA"), sem("sxlB")]
        sxl2p = [sem("sxl2A"), sem("sxl2B")]

        with ExitStack() as w1:
            sbw = lambda n, s, d: w1.enter_context(nc.sbuf_tensor(n, s, d))
            sgin = sbw("sgin", [1, 1], F32)
            psg = w1.enter_context(nc.psum_tensor("psg", [128, 1], F32))
            blk = w1.enter_context(nc.Block("w1a"))

            @blk.gpsimd
            def _(g):
                g.dma_start(out=bfc_r, in_=bfc_h.ap()).then_inc(swlg, 16)
                g.dma_start(out=bpr_r, in_=bproj_h.ap()).then_inc(swlg, 16)
                g.dma_start(out=ones_r, in_=ones_h.ap()).then_inc(swlg, 16)

            @blk.sync
            def _(s):
                s.dma_start(out=sgin[:, :], in_=gate_h.ap()).then_inc(swl, 16)
                s.dma_start(out=idn_sb[:, :], in_=idn_h.ap()).then_inc(swl, 16)

            @blk.scalar
            def _(s):
                s.wait_ge(swl, 32)
                s.activation(sig_b, sgin[:, :], AF.Sigmoid).then_inc(sSIG, 1)

            @blk.vector
            def _(v):
                v.memset(eps_col[:, :], 1e-5)
                v.wait_ge(sGMM, 1)
                v.tensor_copy(sgate_col[:, :], psg[:, :]).then_inc(sWCV, 1)

            @blk.tensor
            def _(t):
                t.wait_ge(sSIG, 1)
                t.wait_ge(swlg, 48)
                t.matmul(psg[:, :], ones_r, sig_b, start=True, stop=True).then_inc(sGMM, 1)

        NSLOT = NTILE + LAG + 1
        with ExitStack() as fz:
            sba = lambda n, s, d: fz.enter_context(nc.sbuf_tensor(n, s, d))
            wfcT = sba("wfcT", [128, 8, H], F8)
            wprojT = sba("wprojT", [128, 32, D], F8)
            xqT = sba("xqT", [128, 8, RING * 128], F8)
            xnT = sba("xnT", [128, 8, TOKC], F16)
            wq_sb = sba("wq_sb", [128, 8, M], F16)
            wk_sb = sba("wk_sb", [128, 8, M], F16)
            wv_sb = sba("wv_sb", [128, 8, M], F16)
            dq_sb = sba("dq_sb", [128, TOKC], F16)
            dk_sb = sba("dk_sb", [128, TOKC], F16)
            mask_sb = sba("mask_sb", [128, TPC, TOKC], F16)
            ktT = sba("ktT", [128, 2, TOKC], F16)
            ktc = sba("ktc", [128, TPC, M], F16)
            v_sb = sba("v_sb", [128, TPC, M], F16)
            A_sb = sba("A_sb", [128, TPC, TOKC], F16)
            Za = sba("Za", [128, 2, M], F32)
            qT_ch = [sba(f"qT_ch{i}", [128, 2, TOKC], F16) for i in range(2)]
            sx = [sba(f"sx{i}", [128, D], F32) for i in range(2)]
            sxn = [sba("sxn0", [128, D], F16)] * 2
            sxq = [sba("sxq0", [128, D], F16)] * 2
            xst = [sba("xst0", [128, 8, 128], F16)] * 2
            swf = [sba(f"swf{i}", [128, D], F32) for i in range(2)]
            wq16 = [sba(f"wq16{i}", [128, D], F16) for i in range(2)]
            tst = [sba(f"tst{i}", [128, 8, 128], F16) for i in range(2)]
            h_bf = [sba(f"h_bf{i}", [128, D], F16) for i in range(2)]
            hq16 = [sba(f"hq16{i}", [128, D], F16) for i in range(2)]
            hst16 = [sba(f"hst16{i}", [128, 8, 128], F16) for i in range(2)]
            hstage8 = sba("hstage8", [128, 32, 128], F8)
            ob16 = [sba("ob16_0", [128, D], F16)] * 2
            bst = sba("bst", [128, 2, 6], F32)
            bag = sba("bag", [128, 2], F32)
            slnv = sba("slnv", [128, 1], F32)
            srstd = sba("srstd", [128, 1], F32)
            negmr = sba("negmr", [128, 1], F32)
            habs = sba("a_habs", [128, D], F16)
            l1 = sba("a_l1", [128, 32, 16], F16)
            am_bf = sba("a_am", [128, 32], F16)
            am32 = sba("a_am32", [128, 32], F32)
            acl = sba("a_acl", [128, 32], F32)
            apw = sba("a_apw", [128, 32], F32)
            asc = sba("a_asc", [128, 32], F32)
            ain = sba("a_ain", [128, 32], F32)
            asc16 = sba("a_asc16", [128, 32], F16)
            ain16 = sba("a_ain16", [128, 32], F16)
            rinv = sba("a_rinv", [128, 32], F32)
            nminv = sba("a_nminv", [128, 32], F32)
            P1 = fz.enter_context(nc.psum_tensor("P1", [128, 2048], F32))
            PB = fz.enter_context(nc.psum_tensor("PB", [128, 1024], F32))
            P3 = fz.enter_context(nc.psum_tensor("P3", [128, 512], F32))
            PT = fz.enter_context(nc.psum_tensor("PT", [128, 1024], F16))
            blk = fz.enter_context(nc.Block("fuse"))

            x_t = x_h.ap().transpose([1, 0, 2])
            view3 = lambda a: a.rearrange("p (b e) -> p b e", b=32)
            bc = lambda a: a.unsqueeze(2).broadcast_to([128, 32, 32])

            def w_src(j):
                if j < 32:
                    return wfc_h.ap()[128 * j:128 * (j + 1), :]
                i = j - 32
                T, q = i // 4, i % 4
                return wproj_h.ap()[128 * T:128 * (T + 1), 1024 * q:1024 * (q + 1)]

            SC = lambda t: min(NCHUNK, (t + 1) // 4)

            @blk.sync
            def _(s):
                s.dma_start(out=dq_sb[:, :], in_=dq_h.ap()).then_inc(swl, 16)
                s.dma_start(out=dk_sb[:, :], in_=dk_h.ap()).then_inc(swl, 16)
                s.dma_start(out=mask_sb[:, :, :], in_=mask_h.ap()).then_inc(swl, 16)
                for t in range(NSLOT + 2):
                    if t < NTILE:
                        if t >= 2:
                            s.wait_ge(sXQ, t - 1)
                        s.dma_start(out=sx[t % 2][:, :], in_=x_t[8 * t:8 * (t + 1)]).then_inc(sxlp[t % 2], 16)
                    if t < 8:
                        for j in range(8 * t, 8 * t + 8):
                            if j >= 2:
                                s.wait_ge(sWQD, j - 1)
                            s.dma_start(out=swf[j % 2][:, :], in_=w_src(j)).then_inc(swfl2[j % 2], 16)
                    ts = t - LAG - 2
                    if 0 <= ts < NTILE:
                        s.wait_ge(sPOE, ts + 1)
                        s.dma_start(out=ob_h.ap()[ts], in_=ob16[ts % 2][:, :]).then_inc(sOBS, 16)
                    if t % 4 == 0 and 4 <= t <= NTILE:
                        cp = t // 4 - 1
                        s.wait_ge(sQE, cp + 1)
                        s.dma_start(out=qt_h.ap()[cp], in_=qT_ch[cp % 2][:, :, :]).then_inc(sqts2[cp % 2], 16)
                    th = t - LAG
                    if 0 <= th < NTILE:
                        for pc in range(4):
                            gp = 4 * th + pc
                            s.wait_ge(sHQ, gp + 1)
                            if gp >= 2:
                                s.wait_ge(sH8, gp - 1)
                            s.dma_start(out=hst16[pc % 2][:, :, :],
                                        in_=hq16[pc % 2][:, :].rearrange("p (k c) -> p k c", k=8),
                                        transpose=True).then_inc(shqt2[(4 * th + pc) % 2], 16)

            @blk.gpsimd
            def _(g):
                g.dma_start(out=wq_sb[:, :, :], in_=wqt_h.ap().rearrange("(k p) m -> p k m", p=128)).then_inc(swlg, 16)
                g.dma_start(out=wk_sb[:, :, :], in_=wkt_h.ap().rearrange("(k p) m -> p k m", p=128)).then_inc(swlg, 16)
                g.dma_start(out=wv_sb[:, :, :], in_=wvt_h.ap().rearrange("(k p) m -> p k m", p=128)).then_inc(swlg, 16)
                g.dma_start(out=wot_bf[:, :, :], in_=wot_h.ap().rearrange("(t p) h -> p t h", p=128)).then_inc(swlg, 16)
                g.wait_ge(sZC, 1)
                g.dma_start(out=zin_h.ap().rearrange("(t p) m -> p t m", p=128), in_=zbf[:, :, :]).then_inc(sCCD, 16)
                g.wait_ge(sCCD, 16)
                g.collective_compute(
                    "AllGather", ALU.bypass,
                    replica_groups=[list(range(NCORE))],
                    ins=[zin_h.ap()],
                    outs=[zout_h.ap()],
                ).then_inc(sCC)

            @blk.vector
            def _(v):
                v.memset(Za[:, :, :], 0.0)
                v.wait_ge(swl, 80)
                for t in range(NSLOT):
                    c = t // 4
                    th = t - LAG
                    if t % 4 == 0 and 4 <= t <= NTILE:
                        cp = t // 4 - 1
                        v.wait_ge(sQMM, cp + 1)
                        if cp >= 2:
                            v.wait_ge(sqts2[cp % 2], 16 * (cp // 2))
                        for mt in range(2):
                            v.tensor_tensor(qT_ch[cp % 2][:, mt, :],
                                            P1[:, 512 * mt:512 * (mt + 1)], dq_sb[:, :], ALU.mult)
                        v.engine_nop().then_inc(sQE, 1)
                        v.wait_ge(sKMM, cp + 1)
                        for mt in range(2):
                            v.tensor_tensor(ktT[:, mt, :],
                                            P1[:, 1024 + 512 * mt:1024 + 512 * (mt + 1)], dk_sb[:, :], ALU.mult)
                        v.engine_nop().then_inc(sKE, 1)
                        v.wait_ge(sAMM, cp + 1)
                        for ut in range(TPC):
                            v.tensor_tensor(A_sb[:, ut, :], P1[:, 512 * ut:512 * (ut + 1)],
                                            mask_sb[:, ut, :], ALU.mult)
                        v.engine_nop().then_inc(sAE, 1)
                    if t < NTILE:
                        v.wait_ge(sxlp[t % 2], 16 * (t // 2 + 1))
                        v.bn_stats(bst[:, 0, :], sx[t % 2][:, 0:512])
                        v.bn_stats(bst[:, 1, :], sx[t % 2][:, 512:1024])
                        v.drain()
                        v.bn_aggr(bag[:, :], bst[:, :, :])
                        v.drain()
                        if t >= 1:
                            v.wait_ge(sAPL, t)
                        v.tensor_single_scalar(slnv[:, :], bag[:, 1:2], 1e-5, ALU.add)
                        v.drain()
                        v.tensor_single_scalar(srstd[:, :].bitcast(I32), slnv[:, :].bitcast(I32), 1, ALU.logical_shift_right)
                        v.drain()
                        v.tensor_scalar(srstd[:, :].bitcast(I32), srstd[:, :].bitcast(I32), -1, 0x5F3759DF, ALU.mult, ALU.add)
                        v.drain()
                        for _nw in range(3):
                            v.tensor_tensor(negmr[:, :], slnv[:, :], srstd[:, :], ALU.mult)
                            v.drain()
                            v.tensor_tensor(negmr[:, :], negmr[:, :], srstd[:, :], ALU.mult)
                            v.drain()
                            v.tensor_scalar(negmr[:, :], negmr[:, :], -0.5, 1.5, ALU.mult, ALU.add)
                            v.drain()
                            v.tensor_tensor(srstd[:, :], srstd[:, :], negmr[:, :], ALU.mult)
                            v.drain()
                        v.scalar_tensor_tensor(negmr[:, :], bag[:, 0:1], -1.0, srstd[:, :], ALU.mult, ALU.mult)
                        v.drain()
                        v.engine_nop().then_inc(sNM, 1)
                        v.wait_ge(sAPL, t + 1)
                        if t >= 1:
                            v.wait_ge(sXQT, 16 * t)
                        quant_f32(v, 2, sx[t % 2][:, :], sxq[t % 2][:, :], habs[:, :],
                                  view3, bc, view3, bc,
                                  l1[:, :, :], am_bf[:, :], am32[:, :], acl[:, :],
                                  apw[:, :], asc[:, :], ain[:, :], asc16[:, :],
                                  amax_src16=sxn[t % 2][:, :],
                                  fuse=(srstd[:, :], negmr[:, :], rinv[:, :], nminv[:, :]))
                        v.engine_nop().then_inc(sXQ, 1)
                    if t < 8:
                        for j in range(8 * t, 8 * t + 8):
                            v.wait_ge(swfl2[j % 2], 16 * (j // 2 + 1))
                            if j >= 2:
                                v.wait_ge(swt2[j % 2], 16 * (j // 2))
                            quant_f32(v, 2, swf[j % 2][:, :], wq16[j % 2][:, :], habs[:, :],
                                      view3, bc, view3, bc,
                                      l1[:, :, :], am_bf[:, :], am32[:, :], acl[:, :],
                                      apw[:, :], asc[:, :], ain[:, :], asc16[:, :])
                            v.engine_nop().then_inc(sWQD, 1)
                            if 1 <= j <= 32:
                                v.wait_ge(swt2[(j - 1) % 2], 16 * ((j - 1) // 2 + 1))
                                v.tensor_copy(wfcT[:, :, 128 * (j - 1):128 * j], tst[(j - 1) % 2][:, :, :])
                                v.engine_nop().then_inc(sW8, 1)
                    if 0 <= th < NTILE:
                        for pc in range(4):
                            gp = 4 * th + pc
                            v.wait_ge(sGEL, gp + 1)
                            if gp >= 2:
                                v.wait_ge(shqt2[gp % 2], 16 * (gp // 2))
                            quant_chain(v, 2, h_bf[pc % 2][:, :], hq16[pc % 2][:, :], view3, bc,
                                        l1[:, :, :], am_bf[:, :], am32[:, :], acl[:, :],
                                        apw[:, :], asc[:, :], ain[:, :], asc16[:, :], ain16[:, :], habs=None)
                            v.engine_nop().then_inc(sHQ, 1)
                    if t < NTILE:
                        v.wait_ge(sXQT, 16 * (t + 1))
                        if t >= RING:
                            v.wait_ge(sH1, 4 * (t - RING + 1))
                        v.tensor_copy(xqT[:, :, 128 * (t % RING):128 * (t % RING + 1)], xst[t % 2][:, :, :])
                        v.engine_nop().then_inc(sXQ8, 1)
                    if t % 4 == 0 and 4 <= t <= NTILE:
                        cp = t // 4 - 1
                        v.wait_ge(sUE, cp + 1)
                        v.tensor_tensor(Za[:, :, :], Za[:, :, :], U_all[:, :, cp, :], ALU.add)
                        v.drain()
                        v.tensor_single_scalar(Za[:, :, :], Za[:, :, :], DEC32, ALU.mult)
                        v.drain()
                        if cp == NCHUNK - 1:
                            v.tensor_copy(zbf[:, :, :], Za[:, :, :])
                            v.drain()
                            v.engine_nop().then_inc(sZC, 1)

            @blk.scalar
            def _(s):
                for t in range(NSLOT):
                    c = t // 4
                    th = t - LAG
                    th2 = t - LAG - 1
                    if t < NTILE:
                        s.wait_ge(sNM, t + 1)
                        if t >= 1:
                            s.wait_ge(sXNM, t)
                            s.wait_ge(sXQ, t)
                        s.activation(sxn[t % 2][:, :], sx[t % 2][:, :], AF.Identity,
                                     bias=negmr[:, :], scale=srstd[:, :]).then_inc(sAPL, 1)
                    if t % 4 == 0 and 4 <= t <= NTILE:
                        cp = t // 4 - 1
                        s.wait_ge(sVMM, cp + 1)
                        s.activation(v_sb[:, :, :], PB[:, :].rearrange("p (a b) -> p a b", a=TPC), AF.Identity).then_inc(sVE, 1)
                        s.wait_ge(sKE, cp + 1)
                        if cp >= 1:
                            s.wait_ge(sUMM, cp)
                        for nt in range(2):
                            s.dma_start(out=ktc[:, :, 128 * nt:128 * (nt + 1)],
                                        in_=ktT[:, nt, :].rearrange("p (u c2) -> p u c2", u=TPC),
                                        transpose=True).then_inc(sKTC, 16)
                        s.wait_ge(sRMM, cp + 1)
                        s.activation(rT[:, :, TOKC * cp:TOKC * (cp + 1)],
                                     PB[:, :].rearrange("p (a b) -> p a b", a=2), AF.Identity).then_inc(sRE, 1)
                        s.wait_ge(sUMM, cp + 1)
                        s.activation(U_all[:, :, cp, :], P3[:, :].rearrange("p (a b) -> p a b", a=2), AF.Identity).then_inc(sUE, 1)
                    if t < 8:
                        for j in range(8 * t, 8 * t + 8):
                            s.wait_ge(sWQD, j + 1)
                            if j >= 2:
                                jj = j - 2
                                if jj < 32:
                                    s.wait_ge(sW8, jj + 1)
                                else:
                                    s.wait_ge(sWP8, jj - 32 + 1)
                            s.dma_start(out=tst[j % 2][:, :, :],
                                        in_=wq16[j % 2][:, :].rearrange("p (k c) -> p k c", k=8),
                                        transpose=True).then_inc(swt2[j % 2], 16)
                            if j >= 32:
                                i = j - 32
                                T, q = i // 4, i % 4
                                s.wait_ge(swt2[j % 2], 16 * (j // 2 + 1))
                                s.activation(wprojT[:, 8 * q:8 * (q + 1), 128 * T:128 * (T + 1)],
                                             tst[j % 2][:, :, :], AF.Identity).then_inc(sWP8, 1)
                    if 0 <= th < NTILE:
                        gp = 4 * th
                        s.wait_ge(sH1, gp + 1)
                        if gp >= 2:
                            s.wait_ge(sHQ, gp - 1)
                        s.activation(h_bf[0][:, :], P1[:, 0:1024], AF.Gelu).then_inc(sGEL, 1)
                    if 0 <= th2 < NTILE:
                        for pc in range(2, 4):
                            gp2 = 4 * th2 + pc
                            s.wait_ge(shqt2[gp2 % 2], 16 * (gp2 // 2 + 1))
                            s.wait_ge(sPOM, th2)
                            s.activation(hstage8[:, 8 * pc:8 * (pc + 1), :],
                                         hst16[pc % 2][:, :, :], AF.Identity).then_inc(sH8, 1)
                    if 0 <= th < NTILE:
                        gp = 4 * th + 1
                        s.wait_ge(sH1, gp + 1)
                        if gp >= 2:
                            s.wait_ge(sHQ, gp - 1)
                        s.activation(h_bf[1][:, :], P1[:, 1024:2048], AF.Gelu).then_inc(sGEL, 1)
                    if 0 <= th2 < NTILE:
                        s.wait_ge(sPOM, th2 + 1)
                        if th2 >= 1:
                            s.wait_ge(sOBS, 16 * th2)
                        s.activation(ob16[th2 % 2][:, :], PB[:, :], AF.Identity).then_inc(sPOE, 1)
                    if 0 <= th < NTILE:
                        gp = 4 * th + 2
                        s.wait_ge(sH1, gp + 1)
                        s.wait_ge(sHQ, gp - 1)
                        s.activation(h_bf[0][:, :], P1[:, 0:1024], AF.Gelu).then_inc(sGEL, 1)
                        for pc in range(2):
                            gpc = 4 * th + pc
                            s.wait_ge(shqt2[gpc % 2], 16 * (gpc // 2 + 1))
                            s.wait_ge(sPOM, th)
                            s.activation(hstage8[:, 8 * pc:8 * (pc + 1), :],
                                         hst16[pc % 2][:, :, :], AF.Identity).then_inc(sH8, 1)
                        gp = 4 * th + 3
                        s.wait_ge(sH1, gp + 1)
                        s.wait_ge(sHQ, gp - 1)
                        s.activation(h_bf[1][:, :], P1[:, 1024:2048], AF.Gelu).then_inc(sGEL, 1)
                    if t < NTILE:
                        s.wait_ge(sXQ, t + 1)
                        if t >= 1:
                            s.wait_ge(sXQ8, t)
                        s.dma_start(out=xst[t % 2][:, :, :],
                                    in_=sxq[t % 2][:, :].rearrange("p (k c2) -> p k c2", k=8),
                                    transpose=True).then_inc(sXQT, 16)
                        s.wait_ge(sXNM, t + 1)
                        if t % 4 == 0 and c >= 1:
                            s.wait_ge(sVMM, c)
                        s.activation(xnT[:, :, 128 * (t % 4):128 * (t % 4 + 1)],
                                     PT[:, :].rearrange("p (k c2) -> p k c2", k=8),
                                     AF.Identity).then_inc(sXNE, 1)

            @blk.tensor
            def _(t_):
                t_.wait_ge(sWCV, 1)
                t_.wait_ge(swlg, 112)
                for t in range(NSLOT):
                    c = t // 4
                    th = t - LAG
                    th2 = t - LAG - 1
                    SCv = min(NCHUNK, t // 4)
                    if 0 <= th < NTILE:
                        t_.wait_ge(sXQ8, th + 1)
                        if th == 0:
                            t_.wait_ge(sW8, 32)
                        t_.wait_ge(sAE, SCv)
                        for pc in range(2):
                            gp = 4 * th + pc
                            if gp >= 2:
                                t_.wait_ge(sGEL, gp - 1)
                            off = 1024 * (pc % 2)
                            for k in range(4):
                                for nn in range(2):
                                    t_.matmul(P1[:, off + 512 * nn:off + 512 * (nn + 1)],
                                              xqT[:, 2 * k:2 * k + 2, 128 * (th % RING):128 * (th % RING + 1)],
                                              wfcT[:, 2 * k:2 * k + 2, 1024 * pc + 512 * nn:1024 * pc + 512 * (nn + 1)],
                                              start=(k == 0), stop=False, perf_mode=PM)
                            for nn in range(2):
                                i = t_.matmul(P1[:, off + 512 * nn:off + 512 * (nn + 1)], ones_r,
                                              bfc_r[:, 1024 * pc + 512 * nn:1024 * pc + 512 * (nn + 1)],
                                              start=False, stop=True)
                            i.then_inc(sH1, 1)
                    if 0 <= th2 < NTILE:
                        t_.wait_ge(sH8, 4 * (th2 + 1))
                        if th2 == 0:
                            t_.wait_ge(sWP8, 32)
                        if th2 >= 1:
                            t_.wait_ge(sPOE, th2)
                        t_.wait_ge(sRE, SCv)
                        for k in range(16):
                            for nt in range(2):
                                t_.matmul(PB[:, 512 * nt:512 * (nt + 1)], hstage8[:, 2 * k:2 * k + 2, :],
                                          wprojT[:, 2 * k:2 * k + 2, 512 * nt:512 * (nt + 1)],
                                          start=(k == 0), stop=False, perf_mode=PM)
                        for nt in range(2):
                            i = t_.matmul(PB[:, 512 * nt:512 * (nt + 1)], ones_r,
                                          bpr_r[:, 512 * nt:512 * (nt + 1)], start=False, stop=True)
                        i.then_inc(sPOM, 1)
                    if 0 <= th < NTILE:
                        for pc in range(2, 4):
                            gp = 4 * th + pc
                            t_.wait_ge(sGEL, gp - 1)
                            off = 1024 * (pc % 2)
                            for k in range(4):
                                for nn in range(2):
                                    t_.matmul(P1[:, off + 512 * nn:off + 512 * (nn + 1)],
                                              xqT[:, 2 * k:2 * k + 2, 128 * (th % RING):128 * (th % RING + 1)],
                                              wfcT[:, 2 * k:2 * k + 2, 1024 * pc + 512 * nn:1024 * pc + 512 * (nn + 1)],
                                              start=(k == 0), stop=False, perf_mode=PM)
                            for nn in range(2):
                                i = t_.matmul(P1[:, off + 512 * nn:off + 512 * (nn + 1)], ones_r,
                                              bfc_r[:, 1024 * pc + 512 * nn:1024 * pc + 512 * (nn + 1)],
                                              start=False, stop=True)
                            i.then_inc(sH1, 1)
                    if t < NTILE:
                        t_.wait_ge(sAPL, t + 1)
                        if t >= 1:
                            t_.wait_ge(sXNE, t)
                        for k in range(8):
                            i = t_.transpose(PT[:, 128 * k:128 * (k + 1)],
                                             sxn[t % 2][:, 128 * k:128 * (k + 1)], idn_sb[:, :])
                        i.then_inc(sXNM, 1)
                    if t % 4 == 3 and t < NTILE:
                        t_.wait_ge(sXNE, 4 * (c + 1))
                        if c >= 1:
                            t_.wait_ge(sAE, c)
                        if 0 <= th < NTILE:
                            t_.wait_ge(sGEL, 4 * (th + 1))
                        for mt in range(2):
                            for k in range(8):
                                i = t_.matmul(P1[:, 512 * mt:512 * (mt + 1)], wq_sb[:, k, 128 * mt:128 * (mt + 1)],
                                              xnT[:, k, :], start=(k == 0), stop=(k == 7))
                        i.then_inc(sQMM, 1)
                        for mt in range(2):
                            for k in range(8):
                                i = t_.matmul(P1[:, 1024 + 512 * mt:1024 + 512 * (mt + 1)], wk_sb[:, k, 128 * mt:128 * (mt + 1)],
                                              xnT[:, k, :], start=(k == 0), stop=(k == 7))
                        i.then_inc(sKMM, 1)
                        if c >= 1:
                            t_.wait_ge(sRE, c)
                        if 0 <= th2 < NTILE:
                            t_.wait_ge(sPOE, th2 + 1)
                        for vt in range(TPC):
                            for k in range(8):
                                i = t_.matmul(PB[:, 256 * vt:256 * (vt + 1)], xnT[:, k, 128 * vt:128 * (vt + 1)],
                                              wv_sb[:, k, :], start=(k == 0), stop=(k == 7))
                        i.then_inc(sVMM, 1)
                        t_.wait_ge(sQE, c + 1)
                        t_.wait_ge(sKE, c + 1)
                        for ut in range(TPC):
                            for km in range(2):
                                i = t_.matmul(P1[:, 512 * ut:512 * (ut + 1)], ktT[:, km, 128 * ut:128 * (ut + 1)],
                                              qT_ch[c % 2][:, km, :], start=(km == 0), stop=(km == 1))
                        i.then_inc(sAMM, 1)
                        t_.wait_ge(sAE, c + 1)
                        t_.wait_ge(sVE, c + 1)
                        for mt in range(2):
                            for ut in range(TPC):
                                i = t_.matmul(PB[:, 512 * mt:512 * (mt + 1)], v_sb[:, ut, 128 * mt:128 * (mt + 1)],
                                              A_sb[:, ut, :], start=(ut == 0), stop=(ut == 3))
                        i.then_inc(sRMM, 1)
                        t_.wait_ge(sKTC, 32 * (c + 1))
                        if c >= 1:
                            t_.wait_ge(sUE, c)
                        for nt in range(2):
                            for ut in range(TPC):
                                i = t_.matmul(P3[:, 256 * nt:256 * (nt + 1)], ktc[:, ut, 128 * nt:128 * (nt + 1)],
                                              v_sb[:, ut, :], start=(ut == 0), stop=(ut == 3))
                        i.then_inc(sUMM, 1)

        with ExitStack() as tl:
            sbt = lambda n, s, d: tl.enter_context(nc.sbuf_tensor(n, s, d))
            m0_sb = sbt("m0_sb", [128, 2, M], F32)
            gvt_sb = sbt("gvt_sb", [128, 9], F32)
            Zgt = [sbt(f"Zgt{i}", [128, 2, M], F16) for i in range(2)]
            STa = sbt("STa", [128, 2, M], F32)
            STb = sbt("STb", [128, 2, M], F32)
            st_bf = sbt("st_bf", [128, 2, M], F16)
            qch = [sbt(f"qch{i}", [128, 2, TOKC], F16) for i in range(2)]
            obl = [sbt(f"obl{i}", [128, D], F16) for i in range(2)]
            tnh = sbt("tnh", [128, D], F16)
            g_bf = sbt("g_bf", [128, D], F16)
            sx2 = [sbt(f"sx2_{i}", [128, D], F32) for i in range(2)]
            sout = sbt("sout", [128, D], F32)
            P2 = tl.enter_context(nc.psum_tensor("P2", [128, 1024], F32))
            PS2 = tl.enter_context(nc.psum_tensor("PS2", [128, 1024], F32))
            blk = tl.enter_context(nc.Block("tail"))

            x_t = x_h.ap().transpose([1, 0, 2])
            o_t = out_h.ap().transpose([1, 0, 2])

            @blk.sync
            def _(s):
                s.dma_start(out=m0_sb[:, :, :], in_=m0t_h.ap().rearrange("(t p) m -> p t m", p=128)).then_inc(sTLD, 16)
                s.dma_start(out=gvt_sb[:, :], in_=gvt_h.ap()).then_inc(sTLD, 16)
                for t in range(NTILE):
                    if t >= 2:
                        s.wait_ge(sCMB, t - 1)
                    s.dma_start(out=sx2[t % 2][:, :], in_=x_t[8 * t:8 * (t + 1)]).then_inc(sxl2p[t % 2], 16)
                    if t >= 1:
                        s.wait_ge(sCMB, t)
                        s.dma_start(out=o_t[8 * (t - 1):8 * t], in_=sout[:, :]).then_inc(sOUT, 16)
                s.wait_ge(sCMB, NTILE)
                s.dma_start(out=o_t[8 * (NTILE - 1):8 * NTILE], in_=sout[:, :]).then_inc(sOUT, 16)
                s.wait_ge(sOUT, 16 * NTILE)

            @blk.gpsimd
            def _(g):
                g.wait_ge(sCC, 1)
                for j in range(NCORE):
                    if j >= 2:
                        g.wait_ge(sSTI, j - 1)
                    g.dma_start(out=Zgt[j % 2][:, :, :],
                                in_=zout_h.ap()[M * j:M * (j + 1), :].rearrange("(t p) m -> p t m", p=128)).then_inc(sCCz[j % 2], 16)

            @blk.scalar
            def _(s):
                for c in range(2):
                    s.wait_ge(sqts2[c % 2], 16 * (c // 2 + 1))
                    s.dma_start(out=qch[c % 2][:, :, :], in_=qt_h.ap()[c]).then_inc(sqtl2[c % 2], 16)
                for t in range(NTILE):
                    if t % 4 == 1 and t // 4 + 2 < NCHUNK:
                        c = t // 4 + 2
                        s.wait_ge(sqts2[c % 2], 16 * (c // 2 + 1))
                        s.wait_ge(sRIM, c - 1)
                        s.dma_start(out=qch[c % 2][:, :, :], in_=qt_h.ap()[c]).then_inc(sqtl2[c % 2], 16)
                    s.wait_ge(sOBS, 16 * (t + 1))
                    if t >= 2:
                        s.wait_ge(sSIL, t - 1)
                    s.dma_start(out=obl[t % 2][:, :], in_=ob_h.ap()[t]).then_inc(sobl2[t % 2], 16)
                    s.wait_ge(sobl2[t % 2], 16 * (t // 2 + 1))
                    if t >= 1:
                        s.wait_ge(sSIL, t)
                    s.activation(tnh[:, :], obl[t % 2][:, :], AF.Tanh, scale=0.5).then_inc(sTAN, 1)

            @blk.vector
            def _(v):
                v.wait_ge(sTLD, 32)
                v.tensor_scalar(STa[:, :, :], m0_sb[:, :, :], gvt_sb[:, 8:9], None, ALU.mult)
                v.drain()
                for j in range(NCORE):
                    v.wait_ge(sCCz[j % 2], 16 * (j // 2 + 1))
                    v.tensor_scalar(STb[:, :, :], Zgt[j % 2][:, :, :], gvt_sb[:, j:j + 1], None, ALU.mult)
                    v.drain()
                    v.engine_nop().then_inc(sSTI, 1)
                    v.tensor_tensor(STa[:, :, :], STa[:, :, :], STb[:, :, :], ALU.add)
                    v.drain()
                for c in range(NCHUNK):
                    v.tensor_copy(st_bf[:, :, :], STa[:, :, :])
                    v.drain()
                    v.engine_nop().then_inc(sSTB, 1)
                    v.tensor_tensor(STb[:, :, :], STa[:, :, :], U_all[:, :, c, :], ALU.add)
                    v.drain()
                    v.tensor_single_scalar(STa[:, :, :], STb[:, :, :], DEC32, ALU.mult)
                    v.drain()
                    v.wait_ge(sRIM, c + 1)
                    for mt in range(2):
                        v.tensor_tensor(rT[:, mt, TOKC * c:TOKC * (c + 1)],
                                        rT[:, mt, TOKC * c:TOKC * (c + 1)],
                                        P2[:, 512 * mt:512 * (mt + 1)], ALU.add)
                    v.engine_nop().then_inc(sRIE, 1)
                    for t in range(4 * c, 4 * c + 4):
                        v.wait_ge(sTAN, t + 1)
                        v.tensor_scalar(tnh[:, :], tnh[:, :], 0.5, 0.5, ALU.mult, ALU.add)
                        v.drain()
                        v.tensor_tensor(g_bf[:, :], tnh[:, :], obl[t % 2][:, :], ALU.mult)
                        v.drain()
                        v.engine_nop().then_inc(sSIL, 1)
                        v.wait_ge(sSMM, t + 1)
                        v.wait_ge(sxl2p[t % 2], 16 * (t // 2 + 1))
                        if t >= 1:
                            v.wait_ge(sOUT, 16 * t)
                        v.scalar_tensor_tensor(sout[:, :], PS2[:, :], sgate_col[:, :], g_bf[:, :], ALU.mult, ALU.mult)
                        v.drain()
                        v.tensor_tensor(sout[:, :], sout[:, :], sx2[t % 2][:, :], ALU.add)
                        v.engine_nop().then_inc(sCMB, 1)

            @blk.tensor
            def _(t_):
                for c in range(NCHUNK):
                    t_.wait_ge(sSTB, c + 1)
                    t_.wait_ge(sqtl2[c % 2], 16 * (c // 2 + 1))
                    if c >= 1:
                        t_.wait_ge(sRIE, c)
                    for mt in range(2):
                        for nt in range(2):
                            i = t_.matmul(P2[:, 512 * mt:512 * (mt + 1)], st_bf[:, nt, 128 * mt:128 * (mt + 1)],
                                          qch[c % 2][:, nt, :], start=(nt == 0), stop=(nt == 1))
                    i.then_inc(sRIM, 1)
                    for t in range(4 * c, 4 * c + 4):
                        t_.wait_ge(sRIE, c + 1)
                        if t >= 1:
                            t_.wait_ge(sCMB, t)
                        for nt in range(2):
                            for mt in range(2):
                                i = t_.matmul(PS2[:, 512 * nt:512 * (nt + 1)], rT[:, mt, 128 * t:128 * (t + 1)],
                                              wot_bf[:, mt, 512 * nt:512 * (nt + 1)], start=(mt == 0), stop=(mt == 1))
                        i.then_inc(sSMM, 1)

    return nc


_NC = None


def kernel(L0_attn_x, L0_attn_Wq, L0_attn_Wk, L0_attn_Wv, L0_attn_Wo, L0_attn_gate,
           L0_attn_M0, L0_ffn_w_fc, L0_ffn_b_fc, L0_ffn_w_proj, L0_ffn_b_proj):
    global _NC
    x = np.ascontiguousarray(L0_attn_x)

    wqt = np.ascontiguousarray(np.asarray(L0_attn_Wq).T)
    wkt = np.ascontiguousarray(np.asarray(L0_attn_Wk).T)
    wvt = np.ascontiguousarray(np.asarray(L0_attn_Wv).T)
    wot = np.ascontiguousarray(np.asarray(L0_attn_Wo).T)
    m0t = np.ascontiguousarray(np.asarray(L0_attn_M0).T)
    gate = np.ascontiguousarray(np.asarray(L0_attn_gate).reshape(1, 1))
    wfc = np.ascontiguousarray(np.asarray(L0_ffn_w_fc))
    bfc = np.ascontiguousarray(np.asarray(L0_ffn_b_fc).reshape(1, H))
    wproj = np.ascontiguousarray(np.asarray(L0_ffn_w_proj))
    bproj = np.ascontiguousarray(np.asarray(L0_ffn_b_proj).reshape(1, D))

    tl = (np.arange(TOKC) // B).astype(np.float64)
    dq = np.ascontiguousarray(np.broadcast_to(
        (np.float32(DEC) ** tl).astype(np.float16), (128, TOKC)))
    dk = np.ascontiguousarray(np.broadcast_to(
        ((np.float32(DEC) ** (-1.0 - tl)) / np.float32(B)).astype(np.float16), (128, TOKC)))
    maskt = np.zeros((128, TPC, TOKC), dtype=np.float16)
    for ut in range(TPC):
        uu = ut * 128 + np.arange(128)
        maskt[:, ut, :] = (tl[uu][:, None] < tl[None, :]).astype(np.float16)
    ones = np.ones((1, 128), np.float32)
    idn = np.eye(128, dtype=np.float16)

    if _NC is None:
        _NC = build_nc()

    in_maps = []
    for i in range(NCORE):
        gv = np.zeros((9,), np.float32)
        for j in range(i):
            gv[j] = np.float32(DEC) ** np.float32(256 * (i - 1 - j))
        gv[8] = np.float32(DEC) ** np.float32(256 * i)
        gvt = np.ascontiguousarray(np.broadcast_to(gv, (128, 9)))
        in_maps.append({
            "x": np.ascontiguousarray(x[:, TLOC * i:TLOC * (i + 1), :]),
            "wqt": wqt, "wkt": wkt, "wvt": wvt, "wot": wot, "m0t": m0t, "gate": gate,
            "wfc": wfc, "bfc": bfc, "wproj": wproj, "bproj": bproj,
            "dq": dq, "dk": dk, "maskt": maskt, "gvt": gvt, "onesr": ones, "idn": idn,
        })
    res = run_bass_kernel_spmd(_NC, in_maps, core_ids=list(range(NCORE)))
    return np.concatenate([np.asarray(res.results[i]["out"]) for i in range(NCORE)], axis=1)


# version 15 (final)
# speedup vs baseline: 1.0334x; 1.0334x over previous
import numpy as np
import ml_dtypes
import concourse.bass as bass
import concourse.mybir as mybir
from concourse.bass_utils import run_bass_kernel_spmd
from contextlib import ExitStack

F32 = mybir.dt.float32
F32R = mybir.dt.float32r
BF16 = mybir.dt.bfloat16
F16 = mybir.dt.float16
F8 = mybir.dt.float8e4
PM = mybir.MatmulPerfMode.DoubleRow
I32 = mybir.dt.int32
I16 = mybir.dt.int16
AF = mybir.ActivationFunctionType
ALU = mybir.AluOpType
AX = mybir.AxisListType

NCORE = 8
B = 16
TLOC = 256
D = 1024
M = 256
H = 4096
CH = 32
NCHUNK = TLOC // CH
TOKC = CH * B
NTILE = (TLOC * B) // 128
LAG = 8
RING = 9
TPC = NTILE // NCHUNK
DEC = 0.99
DEC32 = float(np.float32(DEC) ** np.float32(32))
MAGIC = 12582912.0
EXPMASK = -8388608
CLAMP_MIN = float(2.0 ** -10)
CLAMP_BITS = 0x3A800000
AIN_C = (257 << 23) - (1 << 32)


def quant_f32(v, axis, src32, dst16, habs16, view3, bc, view3_32, bc32, l1, am_bf, am32,
              acl, apw, asc, ain, asc16, extra_mul=None, extra_add=None,
              amax_src16=None, fuse=None):
    if amax_src16 is not None:
        v.tensor_reduce(am32.unsqueeze(2), view3(amax_src16), AX.X, ALU.max, apply_absolute_value=True)
    else:
        v.tensor_reduce(am32.unsqueeze(2), view3_32(src32), AX.X, ALU.max, apply_absolute_value=True)
    v.drain()
    v.tensor_single_scalar(acl.bitcast(I32), am32.bitcast(I32), CLAMP_BITS, ALU.max)
    v.drain()
    v.tensor_single_scalar(apw.bitcast(I32), acl.bitcast(I32), EXPMASK, ALU.bitwise_and)
    v.drain()
    v.tensor_single_scalar(asc.bitcast(I32), apw.bitcast(I32), 3 << 23, ALU.subtract)
    v.drain()
    v.tensor_scalar(ain.bitcast(I32), asc.bitcast(I32), -1, 254 << 23, ALU.mult, ALU.add)
    v.tensor_copy(asc16, asc)
    if fuse is not None:
        srstd_col, negmr_col, rinv_b, nminv_b = fuse
        v.tensor_scalar(rinv_b, ain, srstd_col, None, ALU.mult)
        v.tensor_scalar(nminv_b, ain, negmr_col, None, ALU.mult)
        v.drain()
        extra_mul, extra_add = rinv_b, nminv_b
    else:
        v.drain()
    if extra_mul is not None:
        v.tensor_tensor(view3_32(src32), view3_32(src32), bc32(extra_mul), ALU.mult)
        v.drain()
        v.tensor_tensor(view3_32(src32), view3_32(src32), bc32(extra_add), ALU.add)
        v.drain()
    else:
        v.tensor_tensor(view3_32(src32), view3_32(src32), bc32(ain), ALU.mult)
        v.drain()
    v.tensor_scalar(habs16, src32, MAGIC, MAGIC, ALU.add, ALU.subtract)
    v.drain()
    v.tensor_tensor(view3(dst16), view3(habs16), bc(asc16), ALU.mult)
    v.drain()


def quant_chain(v, axis, src_bf, dst_bf, view3, bc, l1, am_bf, am32, acl, apw, asc, ain,
                asc16=None, ain16=None, habs=None, pre_round_wait=None):
    if habs is None:
        habs = dst_bf
    v.tensor_reduce(am32.unsqueeze(2), view3(src_bf), AX.X, ALU.max, apply_absolute_value=True)
    v.drain()
    v.tensor_single_scalar(acl.bitcast(I32), am32.bitcast(I32), CLAMP_BITS, ALU.max)
    v.drain()
    v.tensor_single_scalar(apw.bitcast(I32), acl.bitcast(I32), EXPMASK, ALU.bitwise_and)
    v.drain()
    v.tensor_single_scalar(asc.bitcast(I32), apw.bitcast(I32), 3 << 23, ALU.subtract)
    v.drain()
    v.tensor_scalar(ain.bitcast(I32), asc.bitcast(I32), -1, 254 << 23, ALU.mult, ALU.add)
    v.tensor_copy(asc16, asc)
    v.drain()
    v.tensor_copy(ain16, ain)
    v.drain()
    v.tensor_tensor(view3(habs), view3(src_bf), bc(ain16), ALU.mult)
    v.drain()
    if pre_round_wait is not None:
        pre_round_wait()
    v.tensor_scalar(src_bf, habs, MAGIC, MAGIC, ALU.add, ALU.subtract)
    v.drain()
    v.tensor_tensor(view3(dst_bf), view3(src_bf), bc(asc16), ALU.mult)
    v.drain()


def build_nc():
    nc = bass.Bass("TRN2", target_bir_lowering=False)

    x_h = nc.dram_tensor("x", [B, TLOC, D], F32, kind="ExternalInput")
    wqt_h = nc.dram_tensor("wqt", [D, M], F32, kind="ExternalInput")
    wkt_h = nc.dram_tensor("wkt", [D, M], F32, kind="ExternalInput")
    wvt_h = nc.dram_tensor("wvt", [D, M], F32, kind="ExternalInput")
    wot_h = nc.dram_tensor("wot", [M, D], F32, kind="ExternalInput")
    m0t_h = nc.dram_tensor("m0t", [M, M], F32, kind="ExternalInput")
    gate_h = nc.dram_tensor("gate", [1, 1], F32, kind="ExternalInput")
    wfc_h = nc.dram_tensor("wfc", [H, D], F32, kind="ExternalInput")
    bfc_h = nc.dram_tensor("bfc", [1, H], F32, kind="ExternalInput")
    wproj_h = nc.dram_tensor("wproj", [D, H], F32, kind="ExternalInput")
    bproj_h = nc.dram_tensor("bproj", [1, D], F32, kind="ExternalInput")
    dq_h = nc.dram_tensor("dq", [128, TOKC], F16, kind="ExternalInput")
    dk_h = nc.dram_tensor("dk", [128, TOKC], F16, kind="ExternalInput")
    mask_h = nc.dram_tensor("maskt", [128, TPC, TOKC], F16, kind="ExternalInput")
    gvt_h = nc.dram_tensor("gvt", [128, 9], F32, kind="ExternalInput")
    ones_h = nc.dram_tensor("onesr", [1, 128], F32, kind="ExternalInput")
    idn_h = nc.dram_tensor("idn", [128, 128], F16, kind="ExternalInput")
    out_h = nc.dram_tensor("out", [B, TLOC, D], F32, kind="ExternalOutput")

    zin_h = nc.dram_tensor("zin", [M, M], F16)
    zout_h = nc.dram_tensor("zout", [NCORE * M, M], F16, addr_space="Shared")
    ob_h = nc.dram_tensor("obst", [NTILE, 128, D], F16)
    qt_h = nc.dram_tensor("qtst", [NCHUNK, 128, 2, TOKC], F16)

    ctx = ExitStack()
    sb = lambda n, s, d: ctx.enter_context(nc.sbuf_tensor(n, s, d))
    sem = lambda n: ctx.enter_context(nc.semaphore(n))

    with ctx:
        rT = sb("rT", [128, 2, NTILE * 128], F16)
        U_all = sb("U_all", [128, 2, NCHUNK, M], F16)
        wot_bf = sb("wot_bf", [128, 2, D], F16)
        zbf = sb("zbf", [128, 2, M], F16)
        barena = sb("barena", [1, 5376], F16)
        sgate_col = sb("sgate_col", [128, 1], F32)
        idn_sb = sb("idn_sb", [128, 128], F16)
        eps_col = sb("eps_col", [128, 1], F32)
        bfc_r = barena[0:1, 0:H]
        bpr_r = barena[0:1, H:H + D]
        ones_r = barena[0:1, H + D:H + D + 128]
        sig_b = barena[0:1, H + D + 128:H + D + 129]

        swl = sem("swl"); sSIG = sem("sSIG"); sGMM = sem("sGMM"); sWCV = sem("sWCV")
        sWQD = sem("sWQD"); sWT = sem("sWT"); sW8 = sem("sW8"); sWP8 = sem("sWP8")
        sBNA = sem("sBNA"); sRSD = sem("sRSD"); sNM = sem("sNM"); sAPL = sem("sAPL")
        sXQ = sem("sXQ"); sXQT = sem("sXQT"); sXQ8 = sem("sXQ8")
        sXNM = sem("sXNM"); sXNE = sem("sXNE")
        sBAG = sem("sBAG"); sSQT = sem("sSQT")
        sQMM = sem("sQMM"); sKMM = sem("sKMM"); sVMM = sem("sVMM"); sAMM = sem("sAMM")
        sRMM = sem("sRMM"); sUMM = sem("sUMM")
        sQE = sem("sQE"); sKE = sem("sKE"); sVE = sem("sVE"); sAE = sem("sAE")
        sRE = sem("sRE"); sUE = sem("sUE"); sKTC = sem("sKTC")
        sH1 = sem("sH1"); sGEL = sem("sGEL"); sHQ = sem("sHQ"); sHQT = sem("sHQT")
        sH8 = sem("sH8"); sPOM = sem("sPOM"); sPOE = sem("sPOE")
        sZC = sem("sZC"); sCCD = sem("sCCD"); sCC = sem("sCC")
        sSTI = sem("sSTI"); sSTB = sem("sSTB"); sRIM = sem("sRIM"); sRIE = sem("sRIE")
        sSMM = sem("sSMM"); sCMB = sem("sCMB"); sOUT = sem("sOUT")
        sTAN = sem("sTAN"); sSIL = sem("sSIL"); sTLD = sem("sTLD"); sOBL = sem("sOBL")
        sQTS = sem("sQTS"); sQTL = sem("sQTL")
        swlg = sem("swlg"); sOBS = sem("sOBS")
        swfl2 = [sem("swflA"), sem("swflB")]
        swt2 = [sem("swtA"), sem("swtB")]
        shqt2 = [sem("shqtA"), sem("shqtB")]
        sqts2 = [sem("sqtsA"), sem("sqtsB")]
        sqtl2 = [sem("sqtlA"), sem("sqtlB")]
        sobl2 = [sem("soblA"), sem("soblB")]
        sCCz = [sem("sCCzA"), sem("sCCzB")]
        sxlp = [sem("sxlA"), sem("sxlB")]
        sxl2p = [sem("sxl2A"), sem("sxl2B")]

        with ExitStack() as w1:
            sbw = lambda n, s, d: w1.enter_context(nc.sbuf_tensor(n, s, d))
            sgin = sbw("sgin", [1, 1], F32)
            psg = w1.enter_context(nc.psum_tensor("psg", [128, 1], F32))
            blk = w1.enter_context(nc.Block("w1a"))

            @blk.gpsimd
            def _(g):
                g.dma_start(out=bfc_r, in_=bfc_h.ap()).then_inc(swlg, 16)
                g.dma_start(out=bpr_r, in_=bproj_h.ap()).then_inc(swlg, 16)
                g.dma_start(out=ones_r, in_=ones_h.ap()).then_inc(swlg, 16)

            @blk.sync
            def _(s):
                s.dma_start(out=sgin[:, :], in_=gate_h.ap()).then_inc(swl, 16)
                s.dma_start(out=idn_sb[:, :], in_=idn_h.ap()).then_inc(swl, 16)

            @blk.scalar
            def _(s):
                s.wait_ge(swl, 32)
                s.activation(sig_b, sgin[:, :], AF.Sigmoid).then_inc(sSIG, 1)

            @blk.vector
            def _(v):
                v.memset(eps_col[:, :], 1e-5)
                v.wait_ge(sGMM, 1)
                v.tensor_copy(sgate_col[:, :], psg[:, :]).then_inc(sWCV, 1)

            @blk.tensor
            def _(t):
                t.wait_ge(sSIG, 1)
                t.wait_ge(swlg, 48)
                t.matmul(psg[:, :], ones_r, sig_b, start=True, stop=True).then_inc(sGMM, 1)

        NSLOT = NTILE + LAG + 1
        with ExitStack() as fz:
            sba = lambda n, s, d: fz.enter_context(nc.sbuf_tensor(n, s, d))
            wfcT = sba("wfcT", [128, 8, H], F8)
            wprojT = sba("wprojT", [128, 32, D], F8)
            xqT = sba("xqT", [128, 8, RING * 128], F8)
            xnT = sba("xnT", [128, 8, TOKC], F16)
            wq_sb = sba("wq_sb", [128, 8, M], F16)
            wk_sb = sba("wk_sb", [128, 8, M], F16)
            wv_sb = sba("wv_sb", [128, 8, M], F16)
            dq_sb = sba("dq_sb", [128, TOKC], F16)
            dk_sb = sba("dk_sb", [128, TOKC], F16)
            mask_sb = sba("mask_sb", [128, TPC, TOKC], F16)
            ktT = sba("ktT", [128, 2, TOKC], F16)
            ktc = sba("ktc", [128, TPC, M], F16)
            v_sb = sba("v_sb", [128, TPC, M], F16)
            A_sb = sba("A_sb", [128, TPC, TOKC], F16)
            Za = sba("Za", [128, 2, M], F32)
            qT_ch = [sba(f"qT_ch{i}", [128, 2, TOKC], F16) for i in range(2)]
            sx = [sba(f"sx{i}", [128, D], F32) for i in range(2)]
            sxn = [sba("sxn0", [128, D], F16)] * 2
            sxq = [sba("sxq0", [128, D], F16)] * 2
            xst = [sba("xst0", [128, 8, 128], F16)] * 2
            swf = [sba(f"swf{i}", [128, D], F32) for i in range(2)]
            wq16 = [sba(f"wq16{i}", [128, D], F16) for i in range(2)]
            tst = [sba(f"tst{i}", [128, 8, 128], F16) for i in range(2)]
            h_bf = [sba(f"h_bf{i}", [128, D], F16) for i in range(2)]
            hq16 = [sba(f"hq16{i}", [128, D], F16) for i in range(2)]
            hst16 = [sba(f"hst16{i}", [128, 8, 128], F16) for i in range(2)]
            hstage8 = sba("hstage8", [128, 32, 128], F8)
            ob16 = [sba("ob16_0", [128, D], F16)] * 2
            bst = sba("bst", [128, 2, 6], F32)
            bag = sba("bag", [128, 2], F32)
            slnv = sba("slnv", [128, 1], F32)
            srstd = sba("srstd", [128, 1], F32)
            negmr = sba("negmr", [128, 1], F32)
            habs = sba("a_habs", [128, D], F16)
            l1 = sba("a_l1", [128, 32, 16], F16)
            am_bf = sba("a_am", [128, 32], F16)
            am32 = sba("a_am32", [128, 32], F32)
            acl = sba("a_acl", [128, 32], F32)
            apw = sba("a_apw", [128, 32], F32)
            asc = sba("a_asc", [128, 32], F32)
            ain = sba("a_ain", [128, 32], F32)
            asc16 = sba("a_asc16", [128, 32], F16)
            ain16 = sba("a_ain16", [128, 32], F16)
            rinv = sba("a_rinv", [128, 32], F32)
            nminv = sba("a_nminv", [128, 32], F32)
            P1 = fz.enter_context(nc.psum_tensor("P1", [128, 2048], F32))
            PB = fz.enter_context(nc.psum_tensor("PB", [128, 1024], F32))
            P3 = fz.enter_context(nc.psum_tensor("P3", [128, 512], F32))
            PT = fz.enter_context(nc.psum_tensor("PT", [128, 1024], F16))
            blk = fz.enter_context(nc.Block("fuse"))

            x_t = x_h.ap().transpose([1, 0, 2])
            view3 = lambda a: a.rearrange("p (b e) -> p b e", b=32)
            bc = lambda a: a.unsqueeze(2).broadcast_to([128, 32, 32])

            def w_src(j):
                if j < 32:
                    return wfc_h.ap()[128 * j:128 * (j + 1), :]
                i = j - 32
                T, q = i // 4, i % 4
                return wproj_h.ap()[128 * T:128 * (T + 1), 1024 * q:1024 * (q + 1)]

            SC = lambda t: min(NCHUNK, (t + 1) // 4)

            @blk.sync
            def _(s):
                s.dma_start(out=dq_sb[:, :], in_=dq_h.ap()).then_inc(swl, 16)
                s.dma_start(out=dk_sb[:, :], in_=dk_h.ap()).then_inc(swl, 16)
                s.dma_start(out=mask_sb[:, :, :], in_=mask_h.ap()).then_inc(swl, 16)
                for t in range(NSLOT + 2):
                    if t < NTILE:
                        if t >= 2:
                            s.wait_ge(sXQ, t - 1)
                        s.dma_start(out=sx[t % 2][:, :], in_=x_t[8 * t:8 * (t + 1)]).then_inc(sxlp[t % 2], 16)
                    if t < 8:
                        for j in range(8 * t, 8 * t + 8):
                            if j >= 2:
                                s.wait_ge(sWQD, j - 1)
                            s.dma_start(out=swf[j % 2][:, :], in_=w_src(j)).then_inc(swfl2[j % 2], 16)
                    ts = t - LAG - 2
                    if 0 <= ts < NTILE:
                        s.wait_ge(sPOE, ts + 1)
                        s.dma_start(out=ob_h.ap()[ts], in_=ob16[ts % 2][:, :]).then_inc(sOBS, 16)
                    if t % 4 == 0 and 4 <= t <= NTILE:
                        cp = t // 4 - 1
                        s.wait_ge(sQE, cp + 1)
                        s.dma_start(out=qt_h.ap()[cp], in_=qT_ch[cp % 2][:, :, :]).then_inc(sqts2[cp % 2], 16)
                    th = t - LAG
                    if 0 <= th < NTILE:
                        for pc in range(4):
                            gp = 4 * th + pc
                            s.wait_ge(sHQ, gp + 1)
                            if gp >= 2:
                                s.wait_ge(sH8, gp - 1)
                            s.dma_start(out=hst16[pc % 2][:, :, :],
                                        in_=hq16[pc % 2][:, :].rearrange("p (k c) -> p k c", k=8),
                                        transpose=True).then_inc(shqt2[(4 * th + pc) % 2], 16)

            @blk.gpsimd
            def _(g):
                g.dma_start(out=wq_sb[:, :, :], in_=wqt_h.ap().rearrange("(k p) m -> p k m", p=128)).then_inc(swlg, 16)
                g.dma_start(out=wk_sb[:, :, :], in_=wkt_h.ap().rearrange("(k p) m -> p k m", p=128)).then_inc(swlg, 16)
                g.dma_start(out=wv_sb[:, :, :], in_=wvt_h.ap().rearrange("(k p) m -> p k m", p=128)).then_inc(swlg, 16)
                g.dma_start(out=wot_bf[:, :, :], in_=wot_h.ap().rearrange("(t p) h -> p t h", p=128)).then_inc(swlg, 16)
                g.wait_ge(sZC, 1)
                g.dma_start(out=zin_h.ap().rearrange("(t p) m -> p t m", p=128), in_=zbf[:, :, :]).then_inc(sCCD, 16)
                g.wait_ge(sCCD, 16)
                g.collective_compute(
                    "AllGather", ALU.bypass,
                    replica_groups=[list(range(NCORE))],
                    ins=[zin_h.ap()],
                    outs=[zout_h.ap()],
                ).then_inc(sCC)

            @blk.vector
            def _(v):
                v.memset(Za[:, :, :], 0.0)
                v.wait_ge(swl, 80)
                for t in range(NSLOT):
                    c = t // 4
                    th = t - LAG
                    if t % 4 == 0 and 4 <= t <= NTILE:
                        cp = t // 4 - 1
                        v.wait_ge(sQMM, cp + 1)
                        if cp >= 2:
                            v.wait_ge(sqts2[cp % 2], 16 * (cp // 2))
                        for mt in range(2):
                            v.tensor_tensor(qT_ch[cp % 2][:, mt, :],
                                            P1[:, 512 * mt:512 * (mt + 1)], dq_sb[:, :], ALU.mult)
                        v.engine_nop().then_inc(sQE, 1)
                        v.wait_ge(sKMM, cp + 1)
                        for mt in range(2):
                            v.tensor_tensor(ktT[:, mt, :],
                                            P1[:, 1024 + 512 * mt:1024 + 512 * (mt + 1)], dk_sb[:, :], ALU.mult)
                        v.engine_nop().then_inc(sKE, 1)
                        v.wait_ge(sAMM, cp + 1)
                        for ut in range(TPC):
                            v.tensor_tensor(A_sb[:, ut, :], P1[:, 512 * ut:512 * (ut + 1)],
                                            mask_sb[:, ut, :], ALU.mult)
                        v.engine_nop().then_inc(sAE, 1)
                    if t < NTILE:
                        v.wait_ge(sxlp[t % 2], 16 * (t // 2 + 1))
                        v.bn_stats(bst[:, 0, :], sx[t % 2][:, 0:512])
                        v.bn_stats(bst[:, 1, :], sx[t % 2][:, 512:1024])
                        v.drain()
                        v.bn_aggr(bag[:, :], bst[:, :, :])
                        v.drain()
                        v.engine_nop().then_inc(sBAG, 1)
                        if t >= 1:
                            v.wait_ge(sAPL, t)
                        v.wait_ge(sSQT, t + 1)
                        v.reciprocal(srstd[:, :], slnv[:, :])
                        v.drain()
                        v.scalar_tensor_tensor(negmr[:, :], bag[:, 0:1], -1.0, srstd[:, :], ALU.mult, ALU.mult)
                        v.drain()
                        v.engine_nop().then_inc(sNM, 1)
                        v.wait_ge(sAPL, t + 1)
                        if t >= 1:
                            v.wait_ge(sXQT, 16 * t)
                        quant_f32(v, 2, sx[t % 2][:, :], sxq[t % 2][:, :], habs[:, :],
                                  view3, bc, view3, bc,
                                  l1[:, :, :], am_bf[:, :], am32[:, :], acl[:, :],
                                  apw[:, :], asc[:, :], ain[:, :], asc16[:, :],
                                  amax_src16=sxn[t % 2][:, :],
                                  fuse=(srstd[:, :], negmr[:, :], rinv[:, :], nminv[:, :]))
                        v.engine_nop().then_inc(sXQ, 1)
                    if t < 8:
                        for j in range(8 * t, 8 * t + 8):
                            v.wait_ge(swfl2[j % 2], 16 * (j // 2 + 1))
                            if j >= 2:
                                v.wait_ge(swt2[j % 2], 16 * (j // 2))
                            quant_f32(v, 2, swf[j % 2][:, :], wq16[j % 2][:, :], habs[:, :],
                                      view3, bc, view3, bc,
                                      l1[:, :, :], am_bf[:, :], am32[:, :], acl[:, :],
                                      apw[:, :], asc[:, :], ain[:, :], asc16[:, :])
                            v.engine_nop().then_inc(sWQD, 1)
                            if 1 <= j <= 32:
                                v.wait_ge(swt2[(j - 1) % 2], 16 * ((j - 1) // 2 + 1))
                                v.tensor_copy(wfcT[:, :, 128 * (j - 1):128 * j], tst[(j - 1) % 2][:, :, :])
                                v.engine_nop().then_inc(sW8, 1)
                    if 0 <= th < NTILE:
                        for pc in range(4):
                            gp = 4 * th + pc
                            v.wait_ge(sGEL, gp + 1)
                            if gp >= 2:
                                v.wait_ge(shqt2[gp % 2], 16 * (gp // 2))
                            quant_chain(v, 2, h_bf[pc % 2][:, :], hq16[pc % 2][:, :], view3, bc,
                                        l1[:, :, :], am_bf[:, :], am32[:, :], acl[:, :],
                                        apw[:, :], asc[:, :], ain[:, :], asc16[:, :], ain16[:, :], habs=None)
                            v.engine_nop().then_inc(sHQ, 1)
                    if t < NTILE:
                        v.wait_ge(sXQT, 16 * (t + 1))
                        if t >= RING:
                            v.wait_ge(sH1, 4 * (t - RING + 1))
                        v.tensor_copy(xqT[:, :, 128 * (t % RING):128 * (t % RING + 1)], xst[t % 2][:, :, :])
                        v.engine_nop().then_inc(sXQ8, 1)
                    if t % 4 == 0 and 4 <= t <= NTILE:
                        cp = t // 4 - 1
                        v.wait_ge(sUE, cp + 1)
                        v.tensor_tensor(Za[:, :, :], Za[:, :, :], U_all[:, :, cp, :], ALU.add)
                        v.drain()
                        v.tensor_single_scalar(Za[:, :, :], Za[:, :, :], DEC32, ALU.mult)
                        v.drain()
                        if cp == NCHUNK - 1:
                            v.tensor_copy(zbf[:, :, :], Za[:, :, :])
                            v.drain()
                            v.engine_nop().then_inc(sZC, 1)

            @blk.scalar
            def _(s):
                for t in range(NSLOT):
                    c = t // 4
                    th = t - LAG
                    th2 = t - LAG - 1
                    if t < NTILE:
                        s.wait_ge(sBAG, t + 1)
                        s.activation(slnv[:, :], bag[:, 1:2], AF.Sqrt,
                                     bias=eps_col[:, :]).then_inc(sSQT, 1)
                        s.wait_ge(sNM, t + 1)
                        if t >= 1:
                            s.wait_ge(sXNM, t)
                            s.wait_ge(sXQ, t)
                        s.activation(sxn[t % 2][:, :], sx[t % 2][:, :], AF.Identity,
                                     bias=negmr[:, :], scale=srstd[:, :]).then_inc(sAPL, 1)
                    if t % 4 == 0 and 4 <= t <= NTILE:
                        cp = t // 4 - 1
                        s.wait_ge(sVMM, cp + 1)
                        s.activation(v_sb[:, :, :], PB[:, :].rearrange("p (a b) -> p a b", a=TPC), AF.Identity).then_inc(sVE, 1)
                        s.wait_ge(sKE, cp + 1)
                        if cp >= 1:
                            s.wait_ge(sUMM, cp)
                        for nt in range(2):
                            s.dma_start(out=ktc[:, :, 128 * nt:128 * (nt + 1)],
                                        in_=ktT[:, nt, :].rearrange("p (u c2) -> p u c2", u=TPC),
                                        transpose=True).then_inc(sKTC, 16)
                        s.wait_ge(sRMM, cp + 1)
                        s.activation(rT[:, :, TOKC * cp:TOKC * (cp + 1)],
                                     PB[:, :].rearrange("p (a b) -> p a b", a=2), AF.Identity).then_inc(sRE, 1)
                        s.wait_ge(sUMM, cp + 1)
                        s.activation(U_all[:, :, cp, :], P3[:, :].rearrange("p (a b) -> p a b", a=2), AF.Identity).then_inc(sUE, 1)
                    if t < 8:
                        for j in range(8 * t, 8 * t + 8):
                            s.wait_ge(sWQD, j + 1)
                            if j >= 2:
                                jj = j - 2
                                if jj < 32:
                                    s.wait_ge(sW8, jj + 1)
                                else:
                                    s.wait_ge(sWP8, jj - 32 + 1)
                            s.dma_start(out=tst[j % 2][:, :, :],
                                        in_=wq16[j % 2][:, :].rearrange("p (k c) -> p k c", k=8),
                                        transpose=True).then_inc(swt2[j % 2], 16)
                            if j >= 32:
                                i = j - 32
                                T, q = i // 4, i % 4
                                s.wait_ge(swt2[j % 2], 16 * (j // 2 + 1))
                                s.activation(wprojT[:, 8 * q:8 * (q + 1), 128 * T:128 * (T + 1)],
                                             tst[j % 2][:, :, :], AF.Identity).then_inc(sWP8, 1)
                    if 0 <= th < NTILE:
                        gp = 4 * th
                        s.wait_ge(sH1, gp + 1)
                        if gp >= 2:
                            s.wait_ge(sHQ, gp - 1)
                        s.activation(h_bf[0][:, :], P1[:, 0:1024], AF.Gelu).then_inc(sGEL, 1)
                    if 0 <= th2 < NTILE:
                        for pc in range(2, 4):
                            gp2 = 4 * th2 + pc
                            s.wait_ge(shqt2[gp2 % 2], 16 * (gp2 // 2 + 1))
                            s.wait_ge(sPOM, th2)
                            s.activation(hstage8[:, 8 * pc:8 * (pc + 1), :],
                                         hst16[pc % 2][:, :, :], AF.Identity).then_inc(sH8, 1)
                    if 0 <= th < NTILE:
                        gp = 4 * th + 1
                        s.wait_ge(sH1, gp + 1)
                        if gp >= 2:
                            s.wait_ge(sHQ, gp - 1)
                        s.activation(h_bf[1][:, :], P1[:, 1024:2048], AF.Gelu).then_inc(sGEL, 1)
                    if 0 <= th2 < NTILE:
                        s.wait_ge(sPOM, th2 + 1)
                        if th2 >= 1:
                            s.wait_ge(sOBS, 16 * th2)
                        s.activation(ob16[th2 % 2][:, :], PB[:, :], AF.Identity).then_inc(sPOE, 1)
                    if 0 <= th < NTILE:
                        gp = 4 * th + 2
                        s.wait_ge(sH1, gp + 1)
                        s.wait_ge(sHQ, gp - 1)
                        s.activation(h_bf[0][:, :], P1[:, 0:1024], AF.Gelu).then_inc(sGEL, 1)
                        for pc in range(2):
                            gpc = 4 * th + pc
                            s.wait_ge(shqt2[gpc % 2], 16 * (gpc // 2 + 1))
                            s.wait_ge(sPOM, th)
                            s.activation(hstage8[:, 8 * pc:8 * (pc + 1), :],
                                         hst16[pc % 2][:, :, :], AF.Identity).then_inc(sH8, 1)
                        gp = 4 * th + 3
                        s.wait_ge(sH1, gp + 1)
                        s.wait_ge(sHQ, gp - 1)
                        s.activation(h_bf[1][:, :], P1[:, 1024:2048], AF.Gelu).then_inc(sGEL, 1)
                    if t < NTILE:
                        s.wait_ge(sXQ, t + 1)
                        if t >= 1:
                            s.wait_ge(sXQ8, t)
                        s.dma_start(out=xst[t % 2][:, :, :],
                                    in_=sxq[t % 2][:, :].rearrange("p (k c2) -> p k c2", k=8),
                                    transpose=True).then_inc(sXQT, 16)
                        s.wait_ge(sXNM, t + 1)
                        if t % 4 == 0 and c >= 1:
                            s.wait_ge(sVMM, c)
                        s.activation(xnT[:, :, 128 * (t % 4):128 * (t % 4 + 1)],
                                     PT[:, :].rearrange("p (k c2) -> p k c2", k=8),
                                     AF.Identity).then_inc(sXNE, 1)

            @blk.tensor
            def _(t_):
                t_.wait_ge(sWCV, 1)
                t_.wait_ge(swlg, 112)
                for t in range(NSLOT):
                    c = t // 4
                    th = t - LAG
                    th2 = t - LAG - 1
                    SCv = min(NCHUNK, t // 4)
                    if 0 <= th < NTILE:
                        t_.wait_ge(sXQ8, th + 1)
                        if th == 0:
                            t_.wait_ge(sW8, 32)
                        t_.wait_ge(sAE, SCv)
                        for pc in range(2):
                            gp = 4 * th + pc
                            if gp >= 2:
                                t_.wait_ge(sGEL, gp - 1)
                            off = 1024 * (pc % 2)
                            for k in range(4):
                                for nn in range(2):
                                    t_.matmul(P1[:, off + 512 * nn:off + 512 * (nn + 1)],
                                              xqT[:, 2 * k:2 * k + 2, 128 * (th % RING):128 * (th % RING + 1)],
                                              wfcT[:, 2 * k:2 * k + 2, 1024 * pc + 512 * nn:1024 * pc + 512 * (nn + 1)],
                                              start=(k == 0), stop=False, perf_mode=PM)
                            for nn in range(2):
                                i = t_.matmul(P1[:, off + 512 * nn:off + 512 * (nn + 1)], ones_r,
                                              bfc_r[:, 1024 * pc + 512 * nn:1024 * pc + 512 * (nn + 1)],
                                              start=False, stop=True)
                            i.then_inc(sH1, 1)
                    if 0 <= th2 < NTILE:
                        t_.wait_ge(sH8, 4 * (th2 + 1))
                        if th2 == 0:
                            t_.wait_ge(sWP8, 32)
                        if th2 >= 1:
                            t_.wait_ge(sPOE, th2)
                        t_.wait_ge(sRE, SCv)
                        for k in range(16):
                            for nt in range(2):
                                t_.matmul(PB[:, 512 * nt:512 * (nt + 1)], hstage8[:, 2 * k:2 * k + 2, :],
                                          wprojT[:, 2 * k:2 * k + 2, 512 * nt:512 * (nt + 1)],
                                          start=(k == 0), stop=False, perf_mode=PM)
                        for nt in range(2):
                            i = t_.matmul(PB[:, 512 * nt:512 * (nt + 1)], ones_r,
                                          bpr_r[:, 512 * nt:512 * (nt + 1)], start=False, stop=True)
                        i.then_inc(sPOM, 1)
                    if 0 <= th < NTILE:
                        for pc in range(2, 4):
                            gp = 4 * th + pc
                            t_.wait_ge(sGEL, gp - 1)
                            off = 1024 * (pc % 2)
                            for k in range(4):
                                for nn in range(2):
                                    t_.matmul(P1[:, off + 512 * nn:off + 512 * (nn + 1)],
                                              xqT[:, 2 * k:2 * k + 2, 128 * (th % RING):128 * (th % RING + 1)],
                                              wfcT[:, 2 * k:2 * k + 2, 1024 * pc + 512 * nn:1024 * pc + 512 * (nn + 1)],
                                              start=(k == 0), stop=False, perf_mode=PM)
                            for nn in range(2):
                                i = t_.matmul(P1[:, off + 512 * nn:off + 512 * (nn + 1)], ones_r,
                                              bfc_r[:, 1024 * pc + 512 * nn:1024 * pc + 512 * (nn + 1)],
                                              start=False, stop=True)
                            i.then_inc(sH1, 1)
                    if t < NTILE:
                        t_.wait_ge(sAPL, t + 1)
                        if t >= 1:
                            t_.wait_ge(sXNE, t)
                        for k in range(8):
                            i = t_.transpose(PT[:, 128 * k:128 * (k + 1)],
                                             sxn[t % 2][:, 128 * k:128 * (k + 1)], idn_sb[:, :])
                        i.then_inc(sXNM, 1)
                    if t % 4 == 3 and t < NTILE:
                        t_.wait_ge(sXNE, 4 * (c + 1))
                        if c >= 1:
                            t_.wait_ge(sAE, c)
                        if 0 <= th < NTILE:
                            t_.wait_ge(sGEL, 4 * (th + 1))
                        for mt in range(2):
                            for k in range(8):
                                i = t_.matmul(P1[:, 512 * mt:512 * (mt + 1)], wq_sb[:, k, 128 * mt:128 * (mt + 1)],
                                              xnT[:, k, :], start=(k == 0), stop=(k == 7))
                        i.then_inc(sQMM, 1)
                        for mt in range(2):
                            for k in range(8):
                                i = t_.matmul(P1[:, 1024 + 512 * mt:1024 + 512 * (mt + 1)], wk_sb[:, k, 128 * mt:128 * (mt + 1)],
                                              xnT[:, k, :], start=(k == 0), stop=(k == 7))
                        i.then_inc(sKMM, 1)
                        if c >= 1:
                            t_.wait_ge(sRE, c)
                        if 0 <= th2 < NTILE:
                            t_.wait_ge(sPOE, th2 + 1)
                        for vt in range(TPC):
                            for k in range(8):
                                i = t_.matmul(PB[:, 256 * vt:256 * (vt + 1)], xnT[:, k, 128 * vt:128 * (vt + 1)],
                                              wv_sb[:, k, :], start=(k == 0), stop=(k == 7))
                        i.then_inc(sVMM, 1)
                        t_.wait_ge(sQE, c + 1)
                        t_.wait_ge(sKE, c + 1)
                        for ut in range(TPC):
                            for km in range(2):
                                i = t_.matmul(P1[:, 512 * ut:512 * (ut + 1)], ktT[:, km, 128 * ut:128 * (ut + 1)],
                                              qT_ch[c % 2][:, km, :], start=(km == 0), stop=(km == 1))
                        i.then_inc(sAMM, 1)
                        t_.wait_ge(sAE, c + 1)
                        t_.wait_ge(sVE, c + 1)
                        for mt in range(2):
                            for ut in range(TPC):
                                i = t_.matmul(PB[:, 512 * mt:512 * (mt + 1)], v_sb[:, ut, 128 * mt:128 * (mt + 1)],
                                              A_sb[:, ut, :], start=(ut == 0), stop=(ut == 3))
                        i.then_inc(sRMM, 1)
                        t_.wait_ge(sKTC, 32 * (c + 1))
                        if c >= 1:
                            t_.wait_ge(sUE, c)
                        for nt in range(2):
                            for ut in range(TPC):
                                i = t_.matmul(P3[:, 256 * nt:256 * (nt + 1)], ktc[:, ut, 128 * nt:128 * (nt + 1)],
                                              v_sb[:, ut, :], start=(ut == 0), stop=(ut == 3))
                        i.then_inc(sUMM, 1)

        with ExitStack() as tl:
            sbt = lambda n, s, d: tl.enter_context(nc.sbuf_tensor(n, s, d))
            m0_sb = sbt("m0_sb", [128, 2, M], F32)
            gvt_sb = sbt("gvt_sb", [128, 9], F32)
            Zgt = [sbt(f"Zgt{i}", [128, 2, M], F16) for i in range(2)]
            STa = sbt("STa", [128, 2, M], F32)
            STb = sbt("STb", [128, 2, M], F32)
            st_bf = sbt("st_bf", [128, 2, M], F16)
            qch = [sbt(f"qch{i}", [128, 2, TOKC], F16) for i in range(2)]
            obl = [sbt(f"obl{i}", [128, D], F16) for i in range(2)]
            tnh = sbt("tnh", [128, D], F16)
            g_bf = sbt("g_bf", [128, D], F16)
            sx2 = [sbt(f"sx2_{i}", [128, D], F32) for i in range(2)]
            sout = sbt("sout", [128, D], F32)
            P2 = tl.enter_context(nc.psum_tensor("P2", [128, 1024], F32))
            PS2 = tl.enter_context(nc.psum_tensor("PS2", [128, 1024], F32))
            blk = tl.enter_context(nc.Block("tail"))

            x_t = x_h.ap().transpose([1, 0, 2])
            o_t = out_h.ap().transpose([1, 0, 2])

            @blk.sync
            def _(s):
                s.dma_start(out=m0_sb[:, :, :], in_=m0t_h.ap().rearrange("(t p) m -> p t m", p=128)).then_inc(sTLD, 16)
                s.dma_start(out=gvt_sb[:, :], in_=gvt_h.ap()).then_inc(sTLD, 16)
                for t in range(NTILE):
                    if t >= 2:
                        s.wait_ge(sCMB, t - 1)
                    s.dma_start(out=sx2[t % 2][:, :], in_=x_t[8 * t:8 * (t + 1)]).then_inc(sxl2p[t % 2], 16)
                    if t >= 1:
                        s.wait_ge(sCMB, t)
                        s.dma_start(out=o_t[8 * (t - 1):8 * t], in_=sout[:, :]).then_inc(sOUT, 16)
                s.wait_ge(sCMB, NTILE)
                s.dma_start(out=o_t[8 * (NTILE - 1):8 * NTILE], in_=sout[:, :]).then_inc(sOUT, 16)
                s.wait_ge(sOUT, 16 * NTILE)

            @blk.gpsimd
            def _(g):
                g.wait_ge(sCC, 1)
                for j in range(NCORE):
                    if j >= 2:
                        g.wait_ge(sSTI, j - 1)
                    g.dma_start(out=Zgt[j % 2][:, :, :],
                                in_=zout_h.ap()[M * j:M * (j + 1), :].rearrange("(t p) m -> p t m", p=128)).then_inc(sCCz[j % 2], 16)

            @blk.scalar
            def _(s):
                for c in range(2):
                    s.wait_ge(sqts2[c % 2], 16 * (c // 2 + 1))
                    s.dma_start(out=qch[c % 2][:, :, :], in_=qt_h.ap()[c]).then_inc(sqtl2[c % 2], 16)
                for t in range(NTILE):
                    if t % 4 == 1 and t // 4 + 2 < NCHUNK:
                        c = t // 4 + 2
                        s.wait_ge(sqts2[c % 2], 16 * (c // 2 + 1))
                        s.wait_ge(sRIM, c - 1)
                        s.dma_start(out=qch[c % 2][:, :, :], in_=qt_h.ap()[c]).then_inc(sqtl2[c % 2], 16)
                    s.wait_ge(sOBS, 16 * (t + 1))
                    if t >= 2:
                        s.wait_ge(sSIL, t - 1)
                    s.dma_start(out=obl[t % 2][:, :], in_=ob_h.ap()[t]).then_inc(sobl2[t % 2], 16)
                    s.wait_ge(sobl2[t % 2], 16 * (t // 2 + 1))
                    if t >= 1:
                        s.wait_ge(sSIL, t)
                    s.activation(tnh[:, :], obl[t % 2][:, :], AF.Tanh, scale=0.5).then_inc(sTAN, 1)

            @blk.vector
            def _(v):
                v.wait_ge(sTLD, 32)
                v.tensor_scalar(STa[:, :, :], m0_sb[:, :, :], gvt_sb[:, 8:9], None, ALU.mult)
                v.drain()
                for j in range(NCORE):
                    v.wait_ge(sCCz[j % 2], 16 * (j // 2 + 1))
                    v.tensor_scalar(STb[:, :, :], Zgt[j % 2][:, :, :], gvt_sb[:, j:j + 1], None, ALU.mult)
                    v.drain()
                    v.engine_nop().then_inc(sSTI, 1)
                    v.tensor_tensor(STa[:, :, :], STa[:, :, :], STb[:, :, :], ALU.add)
                    v.drain()
                for c in range(NCHUNK):
                    v.tensor_copy(st_bf[:, :, :], STa[:, :, :])
                    v.drain()
                    v.engine_nop().then_inc(sSTB, 1)
                    v.tensor_tensor(STb[:, :, :], STa[:, :, :], U_all[:, :, c, :], ALU.add)
                    v.drain()
                    v.tensor_single_scalar(STa[:, :, :], STb[:, :, :], DEC32, ALU.mult)
                    v.drain()
                    v.wait_ge(sRIM, c + 1)
                    for mt in range(2):
                        v.tensor_tensor(rT[:, mt, TOKC * c:TOKC * (c + 1)],
                                        rT[:, mt, TOKC * c:TOKC * (c + 1)],
                                        P2[:, 512 * mt:512 * (mt + 1)], ALU.add)
                    v.engine_nop().then_inc(sRIE, 1)
                    for t in range(4 * c, 4 * c + 4):
                        v.wait_ge(sTAN, t + 1)
                        v.tensor_scalar(tnh[:, :], tnh[:, :], 0.5, 0.5, ALU.mult, ALU.add)
                        v.drain()
                        v.tensor_tensor(g_bf[:, :], tnh[:, :], obl[t % 2][:, :], ALU.mult)
                        v.drain()
                        v.engine_nop().then_inc(sSIL, 1)
                        v.wait_ge(sSMM, t + 1)
                        v.wait_ge(sxl2p[t % 2], 16 * (t // 2 + 1))
                        if t >= 1:
                            v.wait_ge(sOUT, 16 * t)
                        v.scalar_tensor_tensor(sout[:, :], PS2[:, :], sgate_col[:, :], g_bf[:, :], ALU.mult, ALU.mult)
                        v.drain()
                        v.tensor_tensor(sout[:, :], sout[:, :], sx2[t % 2][:, :], ALU.add)
                        v.engine_nop().then_inc(sCMB, 1)

            @blk.tensor
            def _(t_):
                for c in range(NCHUNK):
                    t_.wait_ge(sSTB, c + 1)
                    t_.wait_ge(sqtl2[c % 2], 16 * (c // 2 + 1))
                    if c >= 1:
                        t_.wait_ge(sRIE, c)
                    for mt in range(2):
                        for nt in range(2):
                            i = t_.matmul(P2[:, 512 * mt:512 * (mt + 1)], st_bf[:, nt, 128 * mt:128 * (mt + 1)],
                                          qch[c % 2][:, nt, :], start=(nt == 0), stop=(nt == 1))
                    i.then_inc(sRIM, 1)
                    for t in range(4 * c, 4 * c + 4):
                        t_.wait_ge(sRIE, c + 1)
                        if t >= 1:
                            t_.wait_ge(sCMB, t)
                        for nt in range(2):
                            for mt in range(2):
                                i = t_.matmul(PS2[:, 512 * nt:512 * (nt + 1)], rT[:, mt, 128 * t:128 * (t + 1)],
                                              wot_bf[:, mt, 512 * nt:512 * (nt + 1)], start=(mt == 0), stop=(mt == 1))
                        i.then_inc(sSMM, 1)

    return nc


_NC = None


def kernel(L0_attn_x, L0_attn_Wq, L0_attn_Wk, L0_attn_Wv, L0_attn_Wo, L0_attn_gate,
           L0_attn_M0, L0_ffn_w_fc, L0_ffn_b_fc, L0_ffn_w_proj, L0_ffn_b_proj):
    global _NC
    x = np.ascontiguousarray(L0_attn_x)

    wqt = np.ascontiguousarray(np.asarray(L0_attn_Wq).T)
    wkt = np.ascontiguousarray(np.asarray(L0_attn_Wk).T)
    wvt = np.ascontiguousarray(np.asarray(L0_attn_Wv).T)
    wot = np.ascontiguousarray(np.asarray(L0_attn_Wo).T)
    m0t = np.ascontiguousarray(np.asarray(L0_attn_M0).T)
    gate = np.ascontiguousarray(np.asarray(L0_attn_gate).reshape(1, 1))
    wfc = np.ascontiguousarray(np.asarray(L0_ffn_w_fc))
    bfc = np.ascontiguousarray(np.asarray(L0_ffn_b_fc).reshape(1, H))
    wproj = np.ascontiguousarray(np.asarray(L0_ffn_w_proj))
    bproj = np.ascontiguousarray(np.asarray(L0_ffn_b_proj).reshape(1, D))

    tl = (np.arange(TOKC) // B).astype(np.float64)
    dq = np.ascontiguousarray(np.broadcast_to(
        (np.float32(DEC) ** tl).astype(np.float16), (128, TOKC)))
    dk = np.ascontiguousarray(np.broadcast_to(
        ((np.float32(DEC) ** (-1.0 - tl)) / np.float32(B)).astype(np.float16), (128, TOKC)))
    maskt = np.zeros((128, TPC, TOKC), dtype=np.float16)
    for ut in range(TPC):
        uu = ut * 128 + np.arange(128)
        maskt[:, ut, :] = (tl[uu][:, None] < tl[None, :]).astype(np.float16)
    ones = np.ones((1, 128), np.float32)
    idn = np.eye(128, dtype=np.float16)

    if _NC is None:
        _NC = build_nc()

    in_maps = []
    for i in range(NCORE):
        gv = np.zeros((9,), np.float32)
        for j in range(i):
            gv[j] = np.float32(DEC) ** np.float32(256 * (i - 1 - j))
        gv[8] = np.float32(DEC) ** np.float32(256 * i)
        gvt = np.ascontiguousarray(np.broadcast_to(gv, (128, 9)))
        in_maps.append({
            "x": np.ascontiguousarray(x[:, TLOC * i:TLOC * (i + 1), :]),
            "wqt": wqt, "wkt": wkt, "wvt": wvt, "wot": wot, "m0t": m0t, "gate": gate,
            "wfc": wfc, "bfc": bfc, "wproj": wproj, "bproj": bproj,
            "dq": dq, "dk": dk, "maskt": maskt, "gvt": gvt, "onesr": ones, "idn": idn,
        })
    res = run_bass_kernel_spmd(_NC, in_maps, core_ids=list(range(NCORE)))
    return np.concatenate([np.asarray(res.results[i]["out"]) for i in range(NCORE)], axis=1)
